# version 08 (final)
# baseline (speedup 1.0000x reference)
import numpy as np
import ml_dtypes
import concourse.bass as bass
import concourse.mybir as mybir
from concourse.bass_utils import run_bass_kernel_spmd
from contextlib import ExitStack

F32 = mybir.dt.float32
BF16 = mybir.dt.bfloat16
AF = mybir.ActivationFunctionType
ALU = mybir.AluOpType
AX = mybir.AxisListType
bfdt = ml_dtypes.bfloat16

B, S, HID = 4, 4096, 1024
H = 8
P = 25
NT, NR, NK = 108, 109, 120
NG, GPB, NBLK = 20, 4, 5
T = NG * NT
STORE_T = T - NT
WID = 152
HS = 448
BT = GPB * NT
QS_CONST = float((128.0 ** -0.5) / np.log(2.0))
K_SCALE = float(np.log(1.0 + np.e) / np.log(2.0))
MOD_ORDER = [0, 1, 2, 3, 4, 5, 10, 11, 6, 7, 8, 9]


class Sched:
    QUEUES = ("tensor", "vector", "scalar", "syncq", "gpsq")

    def __init__(self):
        self.qs = {q: [] for q in self.QUEUES}
        self.cnt = {}

    def op(self, queue, fn, deps=(), sem=None, inc=1):
        sem = sem or queue
        self.cnt[sem] = self.cnt.get(sem, 0) + inc
        ev = (sem, self.cnt[sem], queue)
        self.qs[queue].append((fn, tuple(d for d in deps if d is not None), sem, inc))
        return ev

    def emit(self, blk, sems, final_waits):
        def runner(queue):
            def body(eng):
                waited = {}
                for fn, deps, sem, inc in self.qs[queue]:
                    agg = {}
                    for (ds, dv, dq) in deps:
                        if dv > agg.get(ds, 0):
                            agg[ds] = dv
                    for ds, dv in agg.items():
                        if waited.get(ds, 0) < dv:
                            eng.wait_ge(sems[ds], dv)
                            waited[ds] = dv
                    inst = fn(eng)
                    inst.then_inc(sems[sem], inc)
                if queue == "syncq":
                    for (ds, dv, dq) in final_waits:
                        if waited.get(ds, 0) < dv:
                            eng.wait_ge(sems[ds], dv)
                            waited[ds] = dv
            return body

        blk.tensor(runner("tensor"))
        blk.vector(runner("vector"))
        blk.scalar(runner("scalar"))
        blk.sync(runner("syncq"))
        blk.gpsimd(runner("gpsq"))


def build_program():
    nc = bass.Bass("TRN2", target_bir_lowering=False, debug=False)

    x_in = nc.dram_tensor("x_in", [T, HID], F32, kind="ExternalInput").ap()
    w_in = {w: nc.dram_tensor(w, [HID, HID], F32, kind="ExternalInput").ap()
            for w in ("wq", "wk", "wv", "wpost", "wrel", "wr", "wk2", "wv2")}
    wmod_in = nc.dram_tensor("wmod", [HID, 6 * HID], F32, kind="ExternalInput").ap()
    posr_in = nc.dram_tensor("posr", [128, 8 * P], F32, kind="ExternalInput").ap()
    c_in = nc.dram_tensor("c_row", [1, HID], F32, kind="ExternalInput").ap()
    pds_in = nc.dram_tensor("pds", [128, 1], F32, kind="ExternalInput").ap()
    mur_in = nc.dram_tensor("mur", [128, 8], F32, kind="ExternalInput").ap()
    muk_in = nc.dram_tensor("muk", [128, 8], F32, kind="ExternalInput").ap()
    id_in = nc.dram_tensor("ident", [NR, 110], BF16, kind="ExternalInput").ap()
    mask_in = nc.dram_tensor("mask", [NR, H * WID], BF16, kind="ExternalInput").ap()
    kvm_in = nc.dram_tensor("kvm", [128, NG], F32, kind="ExternalInput").ap()
    kvr_in = nc.dram_tensor("kvrow", [1, T + 16], BF16, kind="ExternalInput").ap()
    ind_in = nc.dram_tensor("ind", [1, 240], BF16, kind="ExternalInput").ap()
    mir = nc.dram_tensor("mir", [NR, H * WID], F32, kind="Internal").ap()
    out_hbm = nc.dram_tensor("out", [STORE_T, HID], F32, kind="ExternalOutput").ap()

    sch = Sched()
    stk = ExitStack()
    e = stk.enter_context

    wsb = {w: e(nc.sbuf_tensor("sb_" + w, [128, 8 * 1024], BF16))
           for w in ("wq", "wk", "wv", "wpost", "wr", "wk2", "wv2")}
    mstage = [e(nc.sbuf_tensor(f"mstage{i}", [128, 512], BF16)) for i in range(2)]
    relk = e(nc.sbuf_tensor("relk", [128, H * 32], BF16))
    xt = [e(nc.sbuf_tensor(f"xt{i}", [128, HID], F32)) for i in range(3)]
    xc = xt[2]
    stage = [xt[2]]
    x2t = e(nc.sbuf_tensor("x2t", [128, HID], F32))
    x3t = [e(nc.sbuf_tensor(f"x3t{i}", [128, HID], F32)) for i in range(2)]
    xn = e(nc.sbuf_tensor("xn", [128, HID], BF16))
    xn2 = e(nc.sbuf_tensor("xn2", [128, HID], BF16))
    h1fm = e(nc.sbuf_tensor("h1fm", [128, 8 * HS], BF16))
    qfm = e(nc.sbuf_tensor("qfm", [128, 8 * HS], BF16))
    kfm = e(nc.sbuf_tensor("kfm", [128, 8 * HS], BF16))
    h2fm = [e(nc.sbuf_tensor(f"h2fm{i}", [128, 8 * 112], BF16)) for i in range(2)]
    xkfm = e(nc.sbuf_tensor("xkfm", [128, 8 * 112], BF16))
    kact = e(nc.sbuf_tensor("kact", [128, 8 * 112], BF16))
    xrfm = kact
    krelu = mstage[0]
    vtm = e(nc.sbuf_tensor("vtm", [128, HID], BF16))
    rtm = e(nc.sbuf_tensor("rtm", [128, HID], BF16))
    dense = e(nc.sbuf_tensor("dense", [NR, H * WID], F32))
    tanhd = e(nc.sbuf_tensor("tanhd", [NR, H * WID], F32))
    bde = e(nc.sbuf_tensor("bde", [NR, H * P], F32))
    probs = xn
    probsn = e(nc.sbuf_tensor("probsn", [128, H * 120], BF16))
    posr = probsn
    den = e(nc.sbuf_tensor("den", [NR, H], F32))
    rec = e(nc.sbuf_tensor("rec", [NR, H], F32))
    att = probsn
    avsb = e(nc.sbuf_tensor("avsb", [128, H * 112], BF16))
    smask = e(nc.sbuf_tensor("smask", [NR, H * WID], BF16))
    skvm = e(nc.sbuf_tensor("skvm", [128, NG], F32))
    sident = e(nc.sbuf_tensor("sident", [NR, 110], BF16))
    s1 = e(nc.sbuf_tensor("s1", [128, 1], F32))
    s2 = e(nc.sbuf_tensor("s2", [128, 1], F32))
    mcol = e(nc.sbuf_tensor("mcol", [128, 1], F32))
    mmc = e(nc.sbuf_tensor("mmc", [128, 1], F32))
    e2col = e(nc.sbuf_tensor("e2col", [128, 1], F32))
    varc = e(nc.sbuf_tensor("varc", [128, 1], F32))
    stdc = e(nc.sbuf_tensor("stdc", [128, 1], F32))
    rstd = e(nc.sbuf_tensor("rstd", [128, 1], F32))
    negm = e(nc.sbuf_tensor("negm", [128, 1], F32))
    negmr = e(nc.sbuf_tensor("negmr", [128, 1], F32))
    rstdm = e(nc.sbuf_tensor("rstdm", [128, 1], F32))
    negmrm = e(nc.sbuf_tensor("negmrm", [128, 1], F32))
    epsc = e(nc.sbuf_tensor("epsc", [128, 1], F32))
    onec = e(nc.sbuf_tensor("onec", [128, 1], F32))
    cfm = e(nc.sbuf_tensor("cfm", [128, 8], BF16))
    modrow = x2t
    g1fm = e(nc.sbuf_tensor("g1fm", [128, 8], F32))
    g2fm = e(nc.sbuf_tensor("g2fm", [128, 8], F32))
    safm = e(nc.sbuf_tensor("safm", [128, 8], BF16))
    sbfm = e(nc.sbuf_tensor("sbfm", [128, 8], BF16))
    sbfmf = e(nc.sbuf_tensor("sbfmf", [128, 8], F32))
    growf = dense
    garow = tanhd
    onesrowf = e(nc.sbuf_tensor("onesrowf", [1, 128], F32))
    onerow = e(nc.sbuf_tensor("onerow", [1, 128], BF16))
    rows = {r: e(nc.sbuf_tensor("row_" + r, [1, HID], BF16))
            for r in ("saWk", "saWv", "sbWr", "corrR", "corrK2")}
    csil = rows["corrR"]
    saWqfm = e(nc.sbuf_tensor("saWqfm", [128, 8], F32))
    sbWk2fm = e(nc.sbuf_tensor("sbWk2fm", [128, 8], F32))
    smur = e(nc.sbuf_tensor("smur", [128, 8], F32))
    smuk = e(nc.sbuf_tensor("smuk", [128, 8], F32))
    w1ms = e(nc.sbuf_tensor("w1ms", [128, 8], BF16))
    w1ks = e(nc.sbuf_tensor("w1ks", [128, 8], BF16))
    qscol = e(nc.sbuf_tensor("qscol", [128, 1], F32))
    pdsc = e(nc.sbuf_tensor("pdsc", [128, 1], F32))
    tc1 = e(nc.sbuf_tensor("tc1", [128, 1], F32))
    tc2 = e(nc.sbuf_tensor("tc2", [128, 1], F32))
    sind = e(nc.sbuf_tensor("sind", [1, 240], BF16))
    skvr = e(nc.sbuf_tensor("skvr", [1, T + 16], BF16))

    PS = e(nc.psum_tensor("PS", [128, 3072], F32))
    PT16 = e(nc.psum_tensor("PT16", [128, 896], BF16))
    PM = e(nc.psum_tensor("PM", [128, 512], F32))

    sem_names = list(Sched.QUEUES) + [
        "sld", "sxl0", "sxl1", "sstg0", "sstg1", "smod0", "smod1",
        "sshear", "smir", "sst0", "sst1", "sxc", "sw_wq", "sw_wk", "sw_wv", "sw_wrel",
        "sw_wr", "sw_wk2", "sw_posr"]
    sems = {n: e(nc.semaphore("sem_" + n)) for n in sem_names}

    def vop(fn, deps=()):
        return sch.op("vector", fn, deps)

    def sop(fn, deps=()):
        return sch.op("scalar", fn, deps)

    def top(fn, deps=()):
        return sch.op("tensor", fn, deps)

    def dma(queue, out_ap, in_ap, deps=(), sem=None):
        return sch.op(queue, lambda g: g.dma_start(out=out_ap, in_=in_ap),
                      deps, sem=sem, inc=16)

    ps_busy = {i: None for i in range(8)}

    for dst, src in ((smask, mask_in), (skvm, kvm_in), (sident, id_in),
                     (sind, ind_in), (skvr, kvr_in), (smur, mur_in),
                     (smuk, muk_in), (pdsc, pds_in)):
        ev_small0 = dma("syncq", dst[:], src, sem="sld")

    ev_const = vop(lambda v: (
        v.memset(epsc[:], 1e-5), v.memset(onec[:], 1.0),
        v.memset(onerow[:], 1.0), v.memset(onesrowf[:], 1.0),
        v.memset(h1fm[:], 0.0), v.memset(qfm[:], 0.0), v.memset(kfm[:], 0.0),
        v.memset(h2fm[0][:], 0.0), v.memset(h2fm[1][:], 0.0),
        v.memset(dense[:], 0.0), v.memset(tanhd[:], 0.0),
        v.memset(probsn[:], 0.0), v.memset(avsb[:], 0.0),
        v.memset(kact[:], 0.0), v.memset(xkfm[:], 0.0),
        v.memset(vtm[:], 0.0), v.memset(rtm[:], 0.0),
        v.memset(xt[0][:], 0.0), v.memset(xt[1][:], 0.0), v.memset(xt[2][:], 0.0),
        v.memset(x2t[:], 0.0), v.memset(x3t[0][:], 0.0), v.memset(x3t[1][:], 0.0),
        v.memset(xn[:], 0.0), v.memset(xn2[:], 0.0),
        v.memset(PT16[:].bitcast(F32), 0.0), v.memset(PS[:], 0.0), v.memset(PM[:], 0.0),
        v.memset(den[:], 0.0), v.memset(rec[:], 0.0), v.memset(bde[:], 0.0),
        v.memset(s1[:], 0.0), v.memset(s2[:], 0.0), v.memset(mcol[:], 0.0),
        v.memset(mmc[:], 0.0), v.memset(e2col[:], 0.0), v.memset(varc[:], 0.0),
        v.memset(stdc[:], 1.0), v.memset(rstd[:], 0.0), v.memset(negm[:], 0.0),
        v.memset(negmr[:], 0.0), v.memset(rstdm[:], 0.0), v.memset(negmrm[:], 0.0),
        v.memset(tc1[:], 0.0), v.memset(tc2[:], 0.0), v.memset(qscol[:], 0.0),
        v.memset(g1fm[:], 0.0), v.memset(g2fm[:], 0.0), v.memset(safm[:], 0.0),
        v.memset(sbfm[:], 0.0), v.memset(sbfmf[:], 0.0), v.memset(w1ms[:], 0.0),
        v.memset(w1ks[:], 0.0), v.memset(saWqfm[:], 0.0), v.memset(sbWk2fm[:], 0.0),
        v.memset(cfm[:], 0.0), v.memset(relk[:], 0.0),
        v.memset(rows["saWk"][0:1, :], 0.0), v.memset(rows["saWv"][0:1, :], 0.0),
        v.memset(rows["sbWr"][0:1, :], 0.0), v.memset(rows["corrK2"][0:1, :], 0.0),
        v.memset(mstage[0][:], 0.0), v.memset(mstage[1][:], 0.0))[-1])
    ev_small = dma("syncq", growf[0:1, 0:HID], c_in, deps=[ev_small0, ev_const],
                   sem="sld")
    for _i in range(8):
        ps_busy[_i] = ev_const
    ev_mirz = dma("syncq", mir, tanhd[:], deps=[ev_const], sem="sshear")

    ev_posr = dma("gpsq", posr[:, 0:8 * P], posr_in, deps=[ev_const], sem="sw_posr")
    ev_wld = {}

    def load_weight(w):
        for ct in range(8):
            ev_wld[(w, ct)] = dma("gpsq", wsb[w][:, 1024 * ct:1024 * ct + 1024],
                                  w_in[w][128 * ct:128 * ct + 128, :], sem="sw_" + w)

    ev_sil = sop(lambda s_: s_.activation(out=csil[0:1, :], in_=growf[0:1, 0:HID], func=AF.Silu,
                                          scale=1.0), deps=[ev_small])

    def mk_cfm(t):
        for j in range(8):
            mm = nc.tensor.matmul(out=PM[0:128, 128 + j:129 + j],
                                  lhsT=csil[0:1, 128 * j:128 * j + 128],
                                  rhs=onerow[0:1, 0:1], start=True, stop=True)
        return mm
    ev_cfm_mm = top(mk_cfm, deps=[ev_sil, ev_const])
    ev_cfm = vop(lambda v: v.tensor_copy(out=cfm[:], in_=PM[0:128, 128:136]),
                 deps=[ev_cfm_mm])
    ps_busy[7] = ev_cfm

    load_weight("wq")
    slot_consumer = {0: ev_const, 1: ev_const}
    gabc, gbbc = x3t[0], x3t[1]
    ev_gab = {}
    results = {}
    ev_modfree = None
    mod_idx = 0

    def colize_from(row_ap):
        def fn(t):
            for j in range(8):
                mm = nc.tensor.matmul(out=PM[0:128, 136 + j:137 + j],
                                      lhsT=row_ap[0:1, 128 * j:128 * j + 128],
                                      rhs=onesrowf[0:1, 0:1], start=True, stop=True)
            return mm
        return fn

    for slot_i in range(6):
        ccA, ccB = MOD_ORDER[2 * slot_i], MOD_ORDER[2 * slot_i + 1]
        ev_pair = {}
        for k, cc in enumerate((ccA, ccB)):
            evm_last = None
            for ct in range(8):
                sl = mod_idx % 2
                evl = dma("gpsq", mstage[sl][:],
                          wmod_in[128 * ct:128 * ct + 128, 512 * cc:512 * cc + 512],
                          deps=[slot_consumer[sl]], sem=f"smod{sl}")
                evm_last = top(lambda t, sl=sl, c=ct: nc.tensor.matmul(
                    out=PM[0:1, 0:512], lhsT=cfm[:, c:c + 1], rhs=mstage[sl][:],
                    start=(c == 0), stop=(c == 7)),
                    deps=[evl, ev_cfm] + ([ps_busy[7]] if ct == 0 else []))
                slot_consumer[sl] = evm_last
                mod_idx += 1
            ev_pair[k] = vop(lambda v, k=k: v.tensor_copy(
                out=modrow[0:1, 512 * k:512 * k + 512], in_=PM[0:1, 0:512]),
                deps=[evm_last, ev_modfree])
            ps_busy[7] = ev_pair[k]
        ev_ready = ev_pair[1]
        name = ("sa", "ca", "ga", "gb", "sb", "cb")[slot_i]
        if name == "sa":
            evm = top(colize_from(modrow), deps=[ev_ready, ps_busy[7], ev_const])
            ev_safm = vop(lambda v: v.tensor_copy(out=safm[:], in_=PM[0:128, 136:144]),
                          deps=[evm])
            ps_busy[7] = ev_safm
            results["safm"] = ev_safm
            ev_modfree = evm
        elif name == "ca":
            evr = sop(lambda s_: s_.add(out=growf[0:1, 0:1024], in_=modrow[0:1, :],
                                        add=1.0), deps=[ev_ready])
            evm = top(colize_from(growf), deps=[evr, ps_busy[7]])
            ev_g1fm = vop(lambda v: v.tensor_copy(out=g1fm[:], in_=PM[0:128, 136:144]),
                          deps=[evm])
            ps_busy[7] = ev_g1fm
            results["g1fm"] = ev_g1fm
            ev_modfree = evr
        elif name == "cb":
            evr = sop(lambda s_: s_.add(out=growf[0:1, 0:1024], in_=modrow[0:1, :],
                                        add=1.0), deps=[ev_ready])
            evm = top(colize_from(growf), deps=[evr, ps_busy[7]])
            ev_g2fm = vop(lambda v: v.tensor_copy(out=g2fm[:], in_=PM[0:128, 136:144]),
                          deps=[evm])
            ps_busy[7] = ev_g2fm
            results["g2fm"] = ev_g2fm
            ev_modfree = evr
        elif name == "sb":
            evm = top(colize_from(modrow), deps=[ev_ready, ps_busy[7]])
            ev_sbfm = vop(lambda v: (
                v.tensor_copy(out=sbfm[:], in_=PM[0:128, 136:144]),
                v.tensor_copy(out=sbfmf[:], in_=PM[0:128, 136:144]))[-1], deps=[evm])
            ps_busy[7] = ev_sbfm
            results["sbfm"] = ev_sbfm
            ev_modfree = evm
        else:
            ev_a = sop(lambda s_: s_.activation(out=growf[0:1, 0:1024],
                                                in_=modrow[0:1, :], func=AF.Abs,
                                                scale=1.0), deps=[ev_ready])
            ev_m = vop(lambda v: v.tensor_scalar(out=growf[0:1, 0:1024],
                                                 in0=growf[0:1, 0:1024], scalar1=0.25,
                                                 scalar2=None, op0=ALU.max),
                       deps=[ev_a])
            ev_s = sop(lambda s_: s_.activation(out=garow[0:1, 0:1024],
                                                in_=modrow[0:1, :], func=AF.Sign,
                                                scale=1.0), deps=[ev_a, ev_mirz])
            ev_g = vop(lambda v: v.tensor_tensor(out=garow[0:1, 0:1024],
                                                 in0=garow[0:1, 0:1024],
                                                 in1=growf[0:1, 0:1024], op=ALU.mult),
                       deps=[ev_m, ev_s])
            dst = gabc if name == "ga" else gbbc
            for half in range(2):
                evbm = top(lambda t, hf=half: nc.tensor.matmul(
                    out=PM[0:128, 0:512], lhsT=onesrowf[0:1, 0:128],
                    rhs=garow[0:1, 512 * hf:512 * hf + 512], start=True, stop=True),
                    deps=[ev_g, ps_busy[7]])
                ev_gab[(name, half)] = vop(lambda v, d=dst, hf=half: v.tensor_copy(
                    out=d[:, 512 * hf:512 * hf + 512], in_=PM[0:128, 0:512]),
                    deps=[evbm])
                ps_busy[7] = ev_gab[(name, half)]
            ev_modfree = ev_s
        if slot_i == 1:
            load_weight("wk")
        if slot_i == 3:
            load_weight("wv")
    load_weight("wr")
    load_weight("wk2")
    ev_safm = results["safm"]
    ev_sbfm = results["sbfm"]
    ev_g1fm = results["g1fm"]
    ev_g2fm = results["g2fm"]

    ev_stg = {}
    stg_consumed = {0: ev_const, 1: ev_const}
    ev_wcast = {}
    for i, (w, gnm) in enumerate((("wpost", "ga"), ("wv2", "gb"))):
        bc = gabc if gnm == "ga" else gbbc
        for ct in range(8):
            slot = 0
            ev_stg[(w, ct)] = dma("syncq", stage[slot][:],
                                  w_in[w][128 * ct:128 * ct + 128, :],
                                  deps=[stg_consumed[slot]], sem="sstg0")
            evc = vop(lambda v, ww=w, c=ct, b=bc, sl=slot: v.tensor_tensor(
                out=wsb[ww][:, 1024 * c:1024 * c + 1024], in0=stage[sl][:],
                in1=b[:], op=ALU.mult),
                deps=[ev_stg[(w, ct)], ev_gab[(gnm, 0)], ev_gab[(gnm, 1)]])
            ev_wcast[(w, ct)] = evc
            stg_consumed[slot] = evc
    ev_wpost_done = ev_wcast[("wpost", 7)]
    ev_wv2_done = ev_wcast[("wv2", 7)]

    ev_w1ms0 = vop(lambda v: v.tensor_scalar(out=w1ms[:], in0=smur[:], scalar1=-1.0,
                                             scalar2=1.0, op0=ALU.mult, op1=ALU.add),
                   deps=[ev_small])
    ev_w1ms = vop(lambda v: v.tensor_tensor(out=w1ms[:], in0=w1ms[:], in1=sbfmf[:],
                                            op=ALU.mult), deps=[ev_w1ms0, ev_sbfm])
    ev_w1ks0 = vop(lambda v: v.tensor_scalar(out=w1ks[:], in0=smuk[:], scalar1=-1.0,
                                             scalar2=1.0, op0=ALU.mult, op1=ALU.add),
                   deps=[ev_small])
    ev_w1ks = vop(lambda v: v.tensor_tensor(out=w1ks[:], in0=w1ks[:], in1=sbfmf[:],
                                            op=ALU.mult), deps=[ev_w1ks0, ev_sbfm])

    ev_qs1 = sop(lambda s_: s_.activation(out=tc1[:], in_=pdsc[:], func=AF.Abs,
                                          scale=1.0), deps=[ev_small])
    ev_qs2 = sop(lambda s_: s_.activation(out=tc2[:], in_=tc1[:], func=AF.Exp,
                                          scale=-1.0), deps=[ev_qs1])
    ev_qs3 = sop(lambda s_: s_.activation(out=tc1[:], in_=tc2[:], func=AF.Ln,
                                          bias=onec[:, 0:1], scale=1.0),
                 deps=[ev_qs2, ev_const])
    ev_qs4 = vop(lambda v: v.tensor_scalar(out=tc2[:], in0=pdsc[:], scalar1=0.0,
                                           scalar2=None, op0=ALU.max),
                 deps=[ev_small, ev_qs3])
    ev_qs5 = vop(lambda v: v.tensor_tensor(out=tc2[:], in0=tc2[:], in1=tc1[:],
                                           op=ALU.add), deps=[ev_qs4, ev_qs3])
    ev_qs = vop(lambda v: v.tensor_scalar(out=qscol[:], in0=tc2[:], scalar1=QS_CONST,
                                          scalar2=None, op0=ALU.mult), deps=[ev_qs5])

    wrelb = vtm
    prev_cons = ev_const
    ev_relk = None
    for pas, heads in ((0, (0, 1, 2, 3, 4, 5)), (1, (6, 7))):
        for ct in range(8):
            evl = dma("gpsq", wrelb[:], w_in["wrel"][128 * ct:128 * ct + 128, :],
                      deps=[prev_cons], sem="sw_wrel")

            def mk_relk(t, c=ct, hh=heads):
                for i, h in enumerate(hh):
                    mm = nc.tensor.matmul(out=PS[0:128, 512 * i:512 * i + P],
                                          lhsT=wrelb[:, 128 * h:128 * h + 128],
                                          rhs=posr[:, P * c:P * c + P],
                                          start=(c == 0), stop=(c == 7))
                return mm
            deps = [evl, ev_posr]
            if ct == 0:
                deps += [ps_busy[i] for i in range(len(heads))] + [ev_relk]
            prev_cons = top(mk_relk, deps=deps)
        ev_relk = vop(lambda v, hh=heads: [v.tensor_copy(
            out=relk[:, 32 * h:32 * h + P], in_=PS[0:128, 512 * i:512 * i + P])
            for i, h in enumerate(hh)][-1], deps=[prev_cons])
        for i in range(len(heads)):
            ps_busy[i] = ev_relk

    def beta_row(name, fm_ap, wname, dep_fm):
        last = None
        for half in range(2):
            def mk(t, hf=half, fm=fm_ap, wn=wname):
                for ct in range(8):
                    mm = nc.tensor.matmul(
                        out=PM[0:1, 0:512], lhsT=fm[:, ct:ct + 1],
                        rhs=wsb[wn][:, 1024 * ct + 512 * hf:1024 * ct + 512 * hf + 512],
                        start=(ct == 0), stop=(ct == 7))
                return mm
            deps = [dep_fm, ps_busy[7]] + [ev_wld[(wname, ct)] for ct in range(8)]
            evm = top(mk, deps=deps)
            last = vop(lambda v, hf=half, nm=name: v.tensor_copy(
                out=rows[nm][0:1, 512 * hf:512 * hf + 512], in_=PM[0:1, 0:512]),
                deps=[evm])
            ps_busy[7] = last
        return last

    def beta_fm(fm_ap, wname, dep_fm, out_fm):
        evs = []
        for half in range(2):
            def mk(t, hf=half, fm=fm_ap, wn=wname):
                for ct in range(8):
                    mm = nc.tensor.matmul(
                        out=PM[0:1, 0:512], lhsT=fm[:, ct:ct + 1],
                        rhs=wsb[wn][:, 1024 * ct + 512 * hf:1024 * ct + 512 * hf + 512],
                        start=(ct == 0), stop=(ct == 7))
                return mm
            deps = [dep_fm, ps_busy[7]] + [ev_wld[(wname, ct)] for ct in range(8)]
            evm = top(mk, deps=deps)
            eve = vop(lambda v, hf=half: v.tensor_copy(
                out=garow[0:1, 512 * hf:512 * hf + 512], in_=PM[0:1, 0:512]),
                deps=[evm])
            ps_busy[7] = eve
            evs.append(eve)
        evc = top(colize_from(garow), deps=[evs[-1], ps_busy[7]])
        eve2 = vop(lambda v: v.tensor_copy(out=out_fm[:], in_=PM[0:128, 136:144]),
                   deps=[evc])
        ps_busy[7] = eve2
        return eve2

    ev_row = {"saWk": beta_row("saWk", safm, "wk", ev_safm),
              "saWv": beta_row("saWv", safm, "wv", ev_safm),
              "sbWr": beta_row("sbWr", sbfm, "wr", ev_sbfm),
              "corrR": beta_row("corrR", w1ms, "wr", ev_w1ms),
              "corrK2": beta_row("corrK2", w1ks, "wk2", ev_w1ks)}
    ev_saWqfm = beta_fm(safm, "wq", ev_safm, saWqfm)
    ev_sbWk2fm = beta_fm(sbfm, "wk2", ev_sbfm, sbWk2fm)

    ln_last = [None]

    def ln(x_ap, n_rows, xn_out, sq_dump, g, deps, sq_deps, out_deps):
        prev = ln_last[0]
        ev1 = vop(lambda v: v.tensor_reduce(out=s1[0:n_rows, :], in_=x_ap, axis=AX.X,
                                            op=ALU.add), deps=list(deps) + [prev])
        ev2 = sop(lambda s_: s_.activation(out=sq_dump[0:n_rows, :], in_=x_ap,
                                           func=AF.Square, scale=1.0,
                                           accum_out=s2[0:n_rows, :]),
                  deps=list(deps) + list(sq_deps) + [prev])
        ev3 = vop(lambda v: (
            v.tensor_scalar(out=mcol[0:n_rows, :], in0=s1[0:n_rows, :],
                            scalar1=1.0 / HID, scalar2=None, op0=ALU.mult),
            v.tensor_scalar(out=e2col[0:n_rows, :], in0=s2[0:n_rows, :],
                            scalar1=1.0 / HID, scalar2=None, op0=ALU.mult))[-1],
            deps=[ev1, ev2])
        ev3b = sop(lambda s_: s_.activation(out=mmc[0:n_rows, :], in_=mcol[0:n_rows, :],
                                            func=AF.Square, scale=1.0), deps=[ev3])
        ev3c = vop(lambda v: v.tensor_tensor(out=varc[0:n_rows, :],
                                             in0=e2col[0:n_rows, :],
                                             in1=mmc[0:n_rows, :], op=ALU.subtract),
                   deps=[ev3b])
        ev4 = sop(lambda s_: s_.activation(out=stdc[0:n_rows, :], in_=varc[0:n_rows, :],
                                           func=AF.Sqrt, bias=epsc[0:n_rows, 0:1],
                                           scale=1.0), deps=[ev3c, ev_const])
        ev5a = vop(lambda v: (
            v.reciprocal(out=rstd[0:n_rows, :], in_=stdc[0:n_rows, :]),
            v.tensor_scalar(out=negm[0:n_rows, :], in0=mcol[0:n_rows, :],
                            scalar1=-1.0, scalar2=None, op0=ALU.mult))[-1],
            deps=[ev4])
        ev5b = vop(lambda v: (
            v.tensor_tensor(out=negmr[0:n_rows, :], in0=negm[0:n_rows, :],
                            in1=rstd[0:n_rows, :], op=ALU.mult),
            v.tensor_tensor(out=rstdm[0:n_rows, :], in0=rstd[0:n_rows, :],
                            in1=skvm[0:n_rows, g:g + 1], op=ALU.mult))[-1],
            deps=[ev5a, ev_small])
        ev5c = vop(lambda v: v.tensor_tensor(out=negmrm[0:n_rows, :],
                                             in0=negmr[0:n_rows, :],
                                             in1=skvm[0:n_rows, g:g + 1],
                                             op=ALU.mult), deps=[ev5b])
        ev6 = sop(lambda s_: s_.activation(out=xn_out[0:n_rows, :], in_=x_ap,
                                           func=AF.Identity, bias=negmrm[0:n_rows, 0:1],
                                           scale=rstdm[0:n_rows, 0:1]),
                  deps=[ev5c] + list(out_deps))
        ln_last[0] = ev6
        return ev6

    def to_fm(xn_ap, out_slice_fn, gcol, deps):
        evt = top(lambda t: [nc.tensor.transpose(
            out=PT16[:, 112 * ct:112 * ct + NT],
            in_=xn_ap[0:NT, 128 * ct:128 * ct + 128],
            identity=sident[0:NT, 0:NT]) for ct in range(8)][-1],
            deps=list(deps) + [ps_busy[6], ev_small])
        eve = vop(lambda v: [v.tensor_scalar(
            out=out_slice_fn(ct), in0=PT16[:, 112 * ct:112 * ct + NT],
            scalar1=gcol[:, ct:ct + 1], scalar2=None, op0=ALU.mult)
            for ct in range(8)][-1], deps=[evt])
        ps_busy[6] = eve
        return eve

    ev_xfree = [ev_const, ev_const]
    ev_xc_free = ev_wv2_done
    ev_h1fm_users = None
    ev_qk_users = None
    ev_dense_free = ev_sbWk2fm
    ev_bde_free = None
    ev_probs_free = None
    ev_probsn_free = None
    ev_att_free = None
    ev_av_free = None
    ev_vtm_free = prev_cons
    ev_x2_free = results["g2fm"]
    ev_xn_free = None
    ev_xn2_free = prev_cons
    ev_h2fm_written = [ev_const, ev_const]
    ev_h2fm_free = [None, None]
    ev_mix_free = None
    ev_rtm_free = None
    ev_kact_free = slot_consumer[0]
    ev_x3_free = [ev_wpost_done, ev_wv2_done]
    ev_mir_read = ev_mirz
    ev_tanhd_free = ev_sbWk2fm
    ev_store_last = None

    for blk in range(NBLK):
        bb = blk * BT
        if blk > 0:
            ev_halo = vop(lambda v: v.tensor_copy(
                out=h1fm[:].rearrange("p (c w) -> p c w", c=8)[:, :, 4:16],
                in_=h1fm[:].rearrange("p (c w) -> p c w", c=8)[:, :, 436:448]),
                deps=[ev_h1fm_users])
        else:
            ev_halo = ev_const

        ev_fm_last = None
        for gi in range(GPB):
            g = blk * GPB + gi
            ev_xlg = dma("syncq", xt[gi % 2][0:NT, :], x_in[g * NT:(g + 1) * NT, :],
                         deps=[ev_xfree[gi % 2]], sem=f"sxl{gi % 2}")
            evln = ln(xt[gi % 2][0:NT, :], NT, xn, xn2, g,
                      deps=[ev_xlg, ev_small],
                      sq_deps=[ev_xn2_free], out_deps=[ev_xn_free, ev_probs_free])
            ev_xfree[gi % 2] = evln
            evfm = to_fm(xn, lambda ct, gi=gi: h1fm[:, HS * ct + 16 + NT * gi:
                                                    HS * ct + 16 + NT * gi + NT],
                         g1fm, deps=[evln, ev_halo, ev_g1fm, ev_h1fm_users])
            ev_xn_free = evfm
            ev_fm_last = evfm
        ev_Adone = ev_fm_last

        for ot in range(8):
            bank = ot % 2

            def mk_q(t, o=ot, b=bank):
                for ct in range(8):
                    mm = nc.tensor.matmul(out=PS[0:128, 512 * b:512 * b + 433],
                                          lhsT=wsb["wq"][:, 1024 * ct + 128 * o:
                                                         1024 * ct + 128 * o + 128],
                                          rhs=h1fm[:, HS * ct + 15:HS * ct + 448],
                                          start=(ct == 0), stop=(ct == 7))
                return mm
            deps = [ev_Adone, ps_busy[bank], ev_saWqfm, ev_qk_users] + \
                   [ev_wld[("wq", ct)] for ct in range(8)]
            evm = top(mk_q, deps=deps)
            eve = vop(lambda v, o=ot, b=bank: v.tensor_scalar(
                out=qfm[:, HS * o + 15:HS * o + 448],
                in0=PS[0:128, 512 * b:512 * b + 433],
                scalar1=saWqfm[:, o:o + 1], scalar2=qscol[:, 0:1],
                op0=ALU.add, op1=ALU.mult),
                deps=[evm, ev_qs])
            ps_busy[bank] = eve
        for ot in range(8):
            bank = ot % 2

            def mk_k(t, o=ot, b=bank):
                for ct in range(8):
                    nc.tensor.matmul(out=PS[0:128, 512 * b:512 * b + 444],
                                     lhsT=wsb["wk"][:, 1024 * ct + 128 * o:
                                                    1024 * ct + 128 * o + 128],
                                     rhs=h1fm[:, HS * ct + 4:HS * ct + 448],
                                     start=(ct == 0), stop=False)
                return nc.tensor.matmul(out=PS[0:128, 512 * b:512 * b + 444],
                                        lhsT=rows["saWk"][0:1, 128 * o:128 * o + 128],
                                        rhs=skvr[0:1, 4 + bb:4 + bb + 444],
                                        start=False, stop=True)
            deps = [ps_busy[bank], ev_row["saWk"]] + \
                   [ev_wld[("wk", ct)] for ct in range(8)]
            evm = top(mk_k, deps=deps)
            eve = vop(lambda v, o=ot, b=bank: v.tensor_scalar(
                out=kfm[:, HS * o + 4:HS * o + 448],
                in0=PS[0:128, 512 * b:512 * b + 444],
                scalar1=K_SCALE, scalar2=None, op0=ALU.mult), deps=[evm])
            ps_busy[bank] = eve
        ev_qkdone = ps_busy[1]
        ev_h1fm_users = ev_qkdone

        for gi in range(GPB):
            g = blk * GPB + gi
            base = g * NT
            ind_off = 0 if g == 1 else 120

            def mk_v(t, gi=gi):
                for half in range(2):
                    for ct in range(8):
                        nc.tensor.matmul(
                            out=PS[0:NK, 512 * half:512 * half + 512],
                            lhsT=h1fm[:, HS * ct + 4 + NT * gi:
                                      HS * ct + 4 + NT * gi + NK],
                            rhs=wsb["wv"][:, 1024 * ct + 512 * half:
                                          1024 * ct + 512 * half + 512],
                            start=(ct == 0), stop=False)
                    mm = nc.tensor.matmul(
                        out=PS[0:NK, 512 * half:512 * half + 512],
                        lhsT=skvr[0:1, 4 + base:4 + base + NK],
                        rhs=rows["saWv"][0:1, 512 * half:512 * half + 512],
                        start=False, stop=True)
                return mm
            deps = [ps_busy[0], ps_busy[1], ev_qkdone, ev_row["saWv"],
                    ev_vtm_free] + [ev_wld[("wv", ct)] for ct in range(8)]
            ev_vmm = top(mk_v, deps=deps)
            ev_ve = vop(lambda v: v.tensor_copy(out=vtm[0:NK, :],
                                                in_=PS[0:NK, 0:1024]), deps=[ev_vmm])
            ps_busy[0] = ev_ve
            ps_busy[1] = ev_ve
            ev_h1fm_users = ev_vmm

            def mk_sc(t, gi=gi):
                for h in range(H):
                    nc.tensor.matmul(
                        out=PS[0:NR, 1024 + 256 * h:1024 + 256 * h + NK],
                        lhsT=qfm[:, HS * h + 15 + NT * gi:HS * h + 15 + NT * gi + NR],
                        rhs=kfm[:, HS * h + 4 + NT * gi:HS * h + 4 + NT * gi + NK],
                        start=True, stop=True)
                    mm = nc.tensor.matmul(
                        out=PS[0:NR, 1024 + 256 * h + NK:1024 + 256 * h + NK + P],
                        lhsT=qfm[:, HS * h + 15 + NT * gi:HS * h + 15 + NT * gi + NR],
                        rhs=relk[:, 32 * h:32 * h + P], start=True, stop=True)
                return mm
            ev_sc = top(mk_sc, deps=[ev_qkdone, ev_relk, ps_busy[2], ps_busy[3],
                                     ps_busy[4], ps_busy[5]])
            ev_qk_users = ev_sc
            ev_de = vop(lambda v: v.tensor_copy(
                out=dense[:].rearrange("r (h w) -> r h w", h=H)[:, :, 16:16 + NK],
                in_=PS[0:NR, 1024:3072].rearrange("r (h w) -> r h w", h=H)[:, :, 0:NK]),
                deps=[ev_sc, ev_dense_free])
            ev_be = vop(lambda v: v.tensor_copy(
                out=bde[:].rearrange("r (h p) -> r h p", h=H),
                in_=PS[0:NR, 1024:3072].rearrange("r (h w) -> r h w",
                                                  h=H)[:, :, NK:NK + P]),
                deps=[ev_sc, ev_bde_free])
            for b_ in (2, 3, 4, 5):
                ps_busy[b_] = ev_be
            mirf = mir.rearrange("r w -> (r w)")
            diagA = bass.AP(mirf.tensor, mirf.offset + 15,
                            [[H * WID + 1, NR], [WID, H], [1, P]])
            evA = dma("syncq", diagA, bde[:, :].rearrange("r (h p) -> r h p", h=H),
                      deps=[ev_be, ev_mir_read], sem="sshear")
            diagB = bass.AP(mirf.tensor, mirf.offset + (H * WID + 1) - 10 + 14,
                            [[H * WID + 1, NT], [WID, H], [1, P - 14]])
            evB = dma("syncq", diagB,
                      bde[0:NT, :].rearrange("r (h p) -> r h p", h=H)[:, :, 14:P],
                      deps=[ev_be, ev_mir_read], sem="sshear")
            ev_mirl = dma("syncq", tanhd[:], mir, deps=[evA, evB, ev_tanhd_free],
                          sem="smir")
            ev_bde_free = ev_mirl
            ev_add = vop(lambda v: v.tensor_tensor(out=dense[:], in0=dense[:],
                                                   in1=tanhd[:], op=ALU.add),
                         deps=[ev_mirl, ev_de])
            ev_mir_read = ev_add
            ev_th = sop(lambda s_: s_.activation(out=tanhd[:], in_=dense[:],
                                                 func=AF.Tanh, scale=1.0 / 50.0),
                        deps=[ev_add])
            ev_dense_free = ev_th
            ev_mk = vop(lambda v: v.tensor_tensor(out=tanhd[:], in0=tanhd[:],
                                                  in1=smask[:], op=ALU.add),
                        deps=[ev_th, ev_small])

            def mk_exp(s_):
                for h in range(H):
                    ins = s_.activation(
                        out=probs[0:NR, 120 * h:120 * h + 120],
                        in_=tanhd[:].rearrange("r (h w) -> r h w", h=H)[:, h, 16:136],
                        func=AF.Exp, scale=50.0, accum_out=den[:, h:h + 1])
                return ins
            ev_ex = sop(mk_exp, deps=[ev_mk, ev_probs_free, ev_xn_free])
            ev_tanhd_free = ev_ex
            ev_rc = vop(lambda v: v.reciprocal(out=rec[:], in_=den[:]),
                        deps=[ev_ex])
            ev_nm = vop(lambda v: [v.tensor_scalar(
                out=probsn[0:NR, 120 * h:120 * h + 120],
                in0=probs[0:NR, 120 * h:120 * h + 120],
                scalar1=rec[:, h:h + 1], scalar2=None,
                op0=ALU.mult) for h in range(H)][-1],
                deps=[ev_rc, ev_ex, ev_probsn_free])
            ev_probs_free = ev_nm
            ev_pt = top(lambda t: [nc.tensor.transpose(
                out=PT16[0:NK, 112 * h:112 * h + NR],
                in_=probsn[0:NR, 120 * h:120 * h + 120],
                identity=sident[0:NR, 0:NR]) for h in range(H)][-1],
                deps=[ev_nm, ps_busy[6]])
            ev_probsn_free = ev_pt
            ev_ae = sop(lambda s_: s_.copy(
                out=att[0:NK, 0:H * 112].rearrange("p (h w) -> p h w",
                                                   h=H)[:, :, 0:NR],
                in_=PT16[0:NK, :].rearrange("p (h w) -> p h w", h=H)[:, :, 0:NR]),
                deps=[ev_pt, ev_att_free])
            ps_busy[6] = ev_ae

            def mk_av(t):
                for h in range(H):
                    mm = nc.tensor.matmul(
                        out=PS[0:128, 1024 + 128 * h:1024 + 128 * h + NT],
                        lhsT=vtm[0:NK, 128 * h:128 * h + 128],
                        rhs=att[0:NK, 112 * h + 1:112 * h + NR],
                        start=True, stop=True)
                return mm
            ev_avm = top(mk_av, deps=[ev_ae, ev_ve, ps_busy[2], ps_busy[3]])
            ev_att_free = ev_avm
            ev_vtm_free = ev_avm
            ev_probsn_free = ev_avm
            ev_ave = vop(lambda v: v.tensor_copy(
                out=avsb[:].rearrange("p (h w) -> p h w", h=H)[:, :, 0:NT],
                in_=PS[0:128, 1024:2048].rearrange("p (h w) -> p h w",
                                                   h=H)[:, :, 0:NT]),
                deps=[ev_avm, ev_av_free])
            ps_busy[2] = ev_ave
            ps_busy[3] = ev_ave

            def mk_po(t):
                for half in range(2):
                    for h in range(H):
                        mm = nc.tensor.matmul(
                            out=PS[0:NT, 512 * half:512 * half + 512],
                            lhsT=avsb[:, 112 * h:112 * h + NT],
                            rhs=wsb["wpost"][:, 1024 * h + 512 * half:
                                             1024 * h + 512 * half + 512],
                            start=(h == 0), stop=(h == 7))
                return mm
            ev_pom = top(mk_po, deps=[ev_ave, ev_wpost_done, ps_busy[0],
                                      ps_busy[1]])
            ev_av_free = ev_pom
            ev_xc = dma("syncq", xc[0:NT, :], x_in[g * NT:(g + 1) * NT, :],
                        deps=[ev_xc_free], sem="sxc")
            ev_x2 = vop(lambda v: (
                v.tensor_tensor(out=x2t[0:NT, 0:512], in0=PS[0:NT, 0:512],
                                in1=xc[0:NT, 0:512], op=ALU.add),
                v.tensor_tensor(out=x2t[0:NT, 512:1024], in0=PS[0:NT, 512:1024],
                                in1=xc[0:NT, 512:1024], op=ALU.add))[-1],
                deps=[ev_pom, ev_x2_free, ev_xc])
            ps_busy[0] = ev_x2
            ps_busy[1] = ev_x2
            ev_xc_free = ev_x2

            evln2 = ln(x2t[0:NT, :], NT, xn2, xn, g, deps=[ev_x2],
                       sq_deps=[ev_xn_free, ev_nm], out_deps=[ev_xn2_free])
            par = g % 2
            ev_hl = vop(lambda v, par=par: v.tensor_copy(
                out=h2fm[par][:].rearrange("p (c w) -> p c w", c=8)[:, :, 0:1],
                in_=h2fm[1 - par][:].rearrange("p (c w) -> p c w",
                                               c=8)[:, :, 108:109]),
                deps=[ev_h2fm_free[par], ev_h2fm_written[1 - par]])
            evfm2 = to_fm(xn2, lambda ct, par=par: h2fm[par][:, 112 * ct + 1:
                                                             112 * ct + 1 + NT],
                          g2fm, deps=[evln2, ev_hl, ev_g2fm, ev_h2fm_free[par]])
            ev_xn2_free = evfm2
            ev_h2fm_written[par] = evfm2

            dxm_b, dmu_b = rtm, avsb
            ev_mx1 = vop(lambda v, par=par: v.tensor_tensor(
                out=dxm_b[:].rearrange("p (c w) -> p c w", c=8)[:, :, 0:NT],
                in0=h2fm[par][:].rearrange("p (c w) -> p c w", c=8)[:, :, 1:NR],
                in1=h2fm[par][:].rearrange("p (c w) -> p c w", c=8)[:, :, 0:NT],
                op=ALU.subtract),
                deps=[evfm2, ev_rtm_free])
            ev_mx2 = vop(lambda v: [v.tensor_scalar(
                out=dmu_b[:, 112 * ct:112 * ct + NT],
                in0=dxm_b[:, 128 * ct:128 * ct + NT],
                scalar1=smur[:, ct:ct + 1], scalar2=None, op0=ALU.mult)
                for ct in range(8)][-1],
                deps=[ev_mx1, ev_small, ev_av_free])
            ev_mx3 = vop(lambda v, par=par: v.tensor_tensor(
                out=xrfm[:].rearrange("p (c w) -> p c w", c=8)[:, :, 0:NT],
                in0=dmu_b[:].rearrange("p (c w) -> p c w", c=8)[:, :, 0:NT],
                in1=h2fm[par][:].rearrange("p (c w) -> p c w", c=8)[:, :, 0:NT],
                op=ALU.add), deps=[ev_mx2, ev_mix_free])
            ev_mx4 = vop(lambda v: [v.tensor_scalar(
                out=dmu_b[:, 112 * ct:112 * ct + NT],
                in0=dxm_b[:, 128 * ct:128 * ct + NT],
                scalar1=smuk[:, ct:ct + 1], scalar2=None, op0=ALU.mult)
                for ct in range(8)][-1],
                deps=[ev_mx3])
            ev_mix = vop(lambda v, par=par: v.tensor_tensor(
                out=xkfm[:].rearrange("p (c w) -> p c w", c=8)[:, :, 0:NT],
                in0=dmu_b[:].rearrange("p (c w) -> p c w", c=8)[:, :, 0:NT],
                in1=h2fm[par][:].rearrange("p (c w) -> p c w", c=8)[:, :, 0:NT],
                op=ALU.add), deps=[ev_mx4, ev_mix_free])
            ev_av_free = ev_mix
            ev_rtm_free = ev_mix
            ev_h2fm_free[par] = ev_mix

            def mk_r(t, base=base):
                for half in range(2):
                    for ct in range(8):
                        nc.tensor.matmul(
                            out=PS[0:NT, 1024 + 512 * half:1024 + 512 * half + 512],
                            lhsT=xrfm[:, 112 * ct:112 * ct + NT],
                            rhs=wsb["wr"][:, 1024 * ct + 512 * half:
                                          1024 * ct + 512 * half + 512],
                            start=(ct == 0), stop=False)
                    nc.tensor.matmul(
                        out=PS[0:NT, 1024 + 512 * half:1024 + 512 * half + 512],
                        lhsT=onerow[0:1, 0:NT],
                        rhs=rows["sbWr"][0:1, 512 * half:512 * half + 512],
                        start=False, stop=False)
                    mm = nc.tensor.matmul(
                        out=PS[0:NT, 1024 + 512 * half:1024 + 512 * half + 512],
                        lhsT=sind[0:1, ind_off:ind_off + NT],
                        rhs=rows["corrR"][0:1, 512 * half:512 * half + 512],
                        start=False, stop=True)
                return mm
            deps = [ev_mix, ev_row["sbWr"], ev_row["corrR"], ps_busy[2],
                    ps_busy[3], ev_const] + [ev_wld[("wr", ct)] for ct in range(8)]
            ev_rm = top(mk_r, deps=deps)
            ev_sg = sop(lambda s_: (
                s_.activation(out=rtm[0:NT, 0:512], in_=PS[0:NT, 1024:1536],
                              func=AF.Sigmoid, scale=1.0),
                s_.activation(out=rtm[0:NT, 512:1024], in_=PS[0:NT, 1536:2048],
                              func=AF.Sigmoid, scale=1.0))[-1],
                deps=[ev_rm, ev_rtm_free])
            ps_busy[2] = ev_sg
            ps_busy[3] = ev_sg

            ev_k2_last = None
            ev_kact_parts = []
            for ot in range(8):
                bank = ot % 2

                def mk_k2(t, o=ot, b=bank, base=base):
                    for ct in range(8):
                        nc.tensor.matmul(
                            out=PS[0:128, 512 * b:512 * b + NT],
                            lhsT=wsb["wk2"][:, 1024 * ct + 128 * o:
                                            1024 * ct + 128 * o + 128],
                            rhs=xkfm[:, 112 * ct:112 * ct + NT],
                            start=(ct == 0), stop=False)
                    return nc.tensor.matmul(
                        out=PS[0:128, 512 * b:512 * b + NT],
                        lhsT=rows["corrK2"][0:1, 128 * o:128 * o + 128],
                        rhs=sind[0:1, ind_off:ind_off + NT],
                        start=False, stop=True)
                deps = [ev_mix, ev_sbWk2fm, ev_row["corrK2"], ev_rm,
                        ps_busy[bank]] + [ev_wld[("wk2", ct)] for ct in range(8)]
                ev_k2m = top(mk_k2, deps=deps)
                ev_k2_last = ev_k2m
                ev_rl = sop(lambda s_, o=ot, b=bank: s_.activation(
                    out=krelu[:, 0:NT], in_=PS[0:128, 512 * b:512 * b + NT],
                    func=AF.Relu, scale=1.0, bias=sbWk2fm[:, o:o + 1]),
                    deps=[ev_k2m, ev_kact_free])
                ev_sq = sop(lambda s_, o=ot: s_.activation(
                    out=kact[:, 112 * o:112 * o + NT], in_=krelu[:, 0:NT],
                    func=AF.Square, scale=1.0), deps=[ev_rl])
                ev_kact_free = ev_sq
                ps_busy[bank] = ev_rl
                ev_kact_parts.append(ev_sq)

            def mk_wv(t):
                for half in range(2):
                    for ct in range(8):
                        mm = nc.tensor.matmul(
                            out=PS[0:NT, 1024 + 512 * half:1024 + 512 * half + 512],
                            lhsT=kact[:, 112 * ct:112 * ct + NT],
                            rhs=wsb["wv2"][:, 1024 * ct + 512 * half:
                                           1024 * ct + 512 * half + 512],
                            start=(ct == 0), stop=(ct == 7))
                return mm
            ev_wvm = top(mk_wv, deps=ev_kact_parts + [ev_wv2_done, ps_busy[2],
                                                      ps_busy[3], ev_sg])
            ev_mix_free = ev_wvm
            x3b = x3t[g % 2]
            ev_x3m = vop(lambda v, x3b=x3b: (
                v.tensor_tensor(out=x3b[0:NT, 0:512], in0=PS[0:NT, 1024:1536],
                                in1=rtm[0:NT, 0:512], op=ALU.mult),
                v.tensor_tensor(out=x3b[0:NT, 512:1024], in0=PS[0:NT, 1536:2048],
                                in1=rtm[0:NT, 512:1024], op=ALU.mult))[-1],
                deps=[ev_wvm, ev_x3_free[g % 2], ev_sg])
            ev_x3 = vop(lambda v, x3b=x3b: v.tensor_tensor(
                out=x3b[0:NT, :], in0=x3b[0:NT, :], in1=x2t[0:NT, :], op=ALU.add),
                deps=[ev_x3m, ev_x2])
            ps_busy[2] = ev_x3
            ps_busy[3] = ev_x3
            ev_rtm_free = ev_x3
            ev_x2_free = ev_x3
            if g >= 1:
                ev_st = dma("syncq", out_hbm[(g - 1) * NT:g * NT, :],
                            x3b[0:NT, :], deps=[ev_x3], sem=f"sst{g % 2}")
                ev_store_last = ev_st
                ev_x3_free[g % 2] = ev_st
            else:
                ev_x3_free[g % 2] = ev_x3

    with nc.Block() as blk_:
        sch.emit(blk_, sems, [("sst0", sch.cnt.get("sst0", 0), "syncq"),
                              ("sst1", sch.cnt.get("sst1", 0), "syncq")])
    stk.close()
    return nc


_NC_CACHE = None


def _get_nc():
    global _NC_CACHE
    if _NC_CACHE is None:
        _NC_CACHE = build_program()
    return _NC_CACHE


def kernel(L0_attn_hidden_states, L0_attn_position_embeddings, L0_attn_Wq,
           L0_attn_Wk, L0_attn_Wv, L0_attn_Wpost, L0_attn_Wrel,
           L0_attn_per_dim_scale, L0_ffn_W_r, L0_ffn_W_k, L0_ffn_W_v,
           L0_ffn_mu_r, L0_ffn_mu_k, L0_c, L0_wmod):
    x = np.asarray(L0_attn_hidden_states, dtype=np.float32)
    wT = {k: np.ascontiguousarray(np.asarray(w, np.float32).T) for k, w in
          (("wq", L0_attn_Wq), ("wk", L0_attn_Wk), ("wv", L0_attn_Wv),
           ("wpost", L0_attn_Wpost), ("wrel", L0_attn_Wrel),
           ("wr", L0_ffn_W_r), ("wk2", L0_ffn_W_k), ("wv2", L0_ffn_W_v))}
    wmod = np.ascontiguousarray(np.asarray(L0_wmod, np.float32))
    pos = np.asarray(L0_attn_position_embeddings, np.float32)
    posr = np.ascontiguousarray(
        pos.T.reshape(8, 128, P).transpose(1, 0, 2).reshape(128, 8 * P))
    pds = np.ascontiguousarray(
        np.asarray(L0_attn_per_dim_scale, np.float32).reshape(128, 1))
    mur = np.ascontiguousarray(
        np.asarray(L0_ffn_mu_r, np.float32).reshape(8, 128).T)
    muk = np.ascontiguousarray(
        np.asarray(L0_ffn_mu_k, np.float32).reshape(8, 128).T)
    cmat = np.asarray(L0_c, np.float32)
    ident = np.eye(NR, 110, dtype=bfdt)

    mask = np.full((NR, H, WID), -1000.0, np.float32)
    for r in range(1, NR):
        j = (r - 1) // 12
        mask[r, :, 16 + 12 * j:16 + 12 * j + 24] = 0.0
    mask[0, :, 16] = 0.0
    mask = mask.reshape(NR, H * WID).astype(bfdt)

    in_maps = []
    for b in range(B):
        for sh in range(2):
            tok0 = 2052 * sh - NT
            xs = np.zeros((T, HID), np.float32)
            lo, hi = max(0, tok0), min(S, tok0 + T)
            xs[lo - tok0:hi - tok0] = x[b, lo:hi]
            kvm_full = np.zeros((T,), np.float32)
            kvm_full[lo - tok0:hi - tok0] = 1.0
            kvm = np.zeros((128, NG), np.float32)
            for g in range(NG):
                kvm[0:NT, g] = kvm_full[g * NT:(g + 1) * NT]
            kvrow = np.zeros((1, T + 16), bfdt)
            kvrow[0, 16:16 + T] = kvm_full.astype(bfdt)
            ind = np.zeros((1, 240), bfdt)
            if sh == 0:
                ind[0, 0] = -1.0
            m = {"x_in": xs, "wmod": wmod, "posr": posr,
                 "c_row": np.ascontiguousarray(cmat[b:b + 1]), "pds": pds,
                 "mur": mur, "muk": muk, "ident": ident, "mask": mask,
                 "kvm": kvm, "kvrow": kvrow, "ind": ind}
            m.update(wT)
            in_maps.append(m)

    nc = _get_nc()
    res = run_bass_kernel_spmd(nc, in_maps, core_ids=list(range(8)))

    out = np.empty((B, S, HID), np.float32)
    for b in range(B):
        for sh in range(2):
            o = res.results[2 * b + sh]["out"]
            if sh == 0:
                out[b, 0:2052] = o
            else:
                out[b, 2052:4096] = o[0:2044]
    return out
